# version 08
# baseline (speedup 1.0000x reference)
import numpy as np
import concourse.bass as bass
import concourse.mybir as mybir
from concourse.bass_utils import run_bass_kernel_spmd

F32 = mybir.dt.float32
BF16 = mybir.dt.bfloat16
AF = mybir.ActivationFunctionType
OP = mybir.AluOpType

D = 2048
TOK = 2048
TC = 512
NCH = TOK // TC
NDT = D // 128
II = 8192
NIT = II // 128
C2048 = 1.0 / float(np.sqrt(2048.0))
R2048 = float(np.sqrt(2048.0))
C8192 = 1.0 / float(np.sqrt(8192.0))
EPS = 1e-5

_NC_CACHE = {}


def _had(n):
    h = np.array([[1.0]], dtype=np.float32)
    while h.shape[0] < n:
        h = np.block([[h, h], [h, -h]])
    return np.ascontiguousarray(h.astype(np.float32))


class Prog:

    def __init__(self):
        self.steps = []
        self.cnt = {}

    def n(self, sem):
        return self.cnt.get(id(sem), 0)

    def st(self, eng, fn=None, waits=(), inc=None):
        waits = list(waits)
        if inc is not None and inc[1] == 16:
            waits.append((inc[0], self.n(inc[0])))
        ws = []
        for t in waits:
            if t is None:
                continue
            sem, val = t
            assert val <= self.n(sem), f"forward wait: {val} > {self.n(sem)}"
            if val > 0:
                ws.append((sem, val))
        if inc is not None:
            assert fn is not None
            sem, k = inc
            self.cnt[id(sem)] = self.n(sem) + k
        self.steps.append((eng, fn, ws, inc))

    def tok(self, sem):
        return (sem, self.n(sem))


class DQ:

    def __init__(self, sems):
        self.sems = sems
        self.i = 0

    def nx(self):
        s = self.sems[self.i % len(self.sems)]
        self.i += 1
        return s


def build_nc():
    nc = bass.Bass("TRN2", target_bir_lowering=False)

    uT = nc.dram_tensor("uT", [D, TOK], F32, kind="ExternalInput").ap()
    outT = nc.dram_tensor("outT", [D, TOK], F32, kind="ExternalOutput").ap()
    wupT = nc.dram_tensor("wupT", [D, II], F32, kind="ExternalInput").ap()
    wgaT = nc.dram_tensor("wgaT", [D, II], F32, kind="ExternalInput").ap()
    wdnT = nc.dram_tensor("wdnT", [II, D], F32, kind="ExternalInput").ap()
    wbr = nc.dram_tensor("wbr", [D, 64], F32, kind="ExternalInput").ap()
    wbsT = nc.dram_tensor("wbsT", [256, 64], F32, kind="ExternalInput").ap()
    wcr = nc.dram_tensor("wcr", [256, 64], F32, kind="ExternalInput").ap()
    wcm = nc.dram_tensor("wcm", [64, D], F32, kind="ExternalInput").ap()
    a2 = nc.dram_tensor("a2", [2, 16], F32, kind="ExternalInput").ap()
    pmat = nc.dram_tensor("pmat", [2, 2048], F32, kind="ExternalInput").ap()
    h128 = nc.dram_tensor("h128", [128, 128], F32, kind="ExternalInput").ap()
    suflat = nc.dram_tensor("suflat", [1, 2048], F32, kind="ExternalInput").ap()
    sucols = nc.dram_tensor("sucols", [128, 16], F32, kind="ExternalInput").ap()
    wscu_d = nc.dram_tensor("wscu", [128, 64], F32, kind="ExternalInput").ap()
    wscg_d = nc.dram_tensor("wscg", [128, 64], F32, kind="ExternalInput").ap()
    sudp_d = nc.dram_tensor("sudp", [128, 64], F32, kind="ExternalInput").ap()
    wscdp_d = nc.dram_tensor("wscdp", [128, 16], F32, kind="ExternalInput").ap()
    dcol_d = nc.dram_tensor("dcol", [128, 16], F32, kind="ExternalInput").ap()

    scr_ug = nc.dram_tensor("scr_ug", [NIT, 128, 2 * NDT * 128], BF16, kind="Internal").ap()
    scr_dn = nc.dram_tensor("scr_dn", [NDT, 128, NIT * 128], BF16, kind="Internal").ap()

    import contextlib
    stack = contextlib.ExitStack()
    with stack:
        def SB(name, shape, dtype):
            return stack.enter_context(nc.sbuf_tensor(name, shape, dtype))

        def PSUM(name):
            return stack.enter_context(nc.psum_tensor(name, [128, 512], F32))

        def SEM(name):
            return stack.enter_context(nc.semaphore(name))

        X = SB("X", [128, NDT * TC], F32)
        LN = SB("LN", [128, NDT * TC], BF16)
        FW = SB("FW", [128, NDT * TC], BF16)
        G = SB("G", [128, NIT * TC], BF16)
        WUG = SB("WUG", [128, 2 * 2 * NDT * 128], BF16)
        WD = SB("WD", [128, 2 * NIT * 128], BF16)
        ZT = SB("ZT", [128, 1152], F32)
        STATA = SB("STATA", [1, 2 * TC], F32)
        SIL = SB("SIL", [128, TC], BF16)
        H128B = SB("H128B", [128, 128], BF16)
        SPAD = SB("SPAD", [128, TC], BF16)
        GP = SB("GP", [128, D], BF16)
        WBRB = SB("WBRB", [128, NDT * 64], BF16)
        WSCU = SB("WSCU", [128, 64], F32)
        WSCG = SB("WSCG", [128, 64], F32)
        WSC05 = SB("WSC05", [128, 16], F32)
        DCOL = SB("DCOL", [128, 16], F32)
        SUCB = SB("SUCB", [128, 16], BF16)
        HP = SB("HP", [128, NDT * 128], BF16)
        SROWS = SB("SROWS", [128, NDT * 128], BF16)
        NMRB = SB("NMRB", [128, TC], BF16)
        SREP = SB("SREP", [128, TC], F32)
        NONESB = SB("NONESB", [128, 1], BF16)
        PONES = SB("PONES", [128, 1], BF16)
        P0, P1, P2, P3 = PSUM("P0"), PSUM("P1"), PSUM("P2"), PSUM("P3")
        P4, P5, P6, P7 = PSUM("P4"), PSUM("P5"), PSUM("P6"), PSUM("P7")
        qS, qS2 = SEM("qS"), SEM("qS2")
        qA, qA2 = SEM("qA"), SEM("qA2")
        qG, qG2 = SEM("qG"), SEM("qG2")
        ePE, eS, eV, eG = SEM("ePE"), SEM("eS"), SEM("eV"), SEM("eG")
        block = stack.enter_context(nc.Block())
        PS = [P0, P1, P2, P3, P4, P5, P6, P7]
        p = Prog()
        QS, QA, QG = DQ([qS, qS2]), DQ([qA, qA2]), DQ([qG, qG2])

        def dma(eng, q, fn, waits=()):
            s = q.nx()
            p.st(eng, fn, waits, inc=(s, 16))
            return p.tok(s)

        RST = ZT[:, 0:512]
        NMR = ZT[:, 512:1024]
        ONE1 = ZT[:, 1024:1152]

        Gf = G[:, :].bitcast(F32)
        PT = Gf[:, 0:2048]
        C1 = Gf[:, 2048:4096]
        WCM = Gf[:, 4096:6144]
        MC = Gf[:, 6144:6656]
        WBS = Gf[:, 6656:6784]
        WCR = Gf[:, 6784:6912]
        T1C = Gf[:, 6912:7040]
        KTC = Gf[:, 7040:7104]
        ATC = Gf[:, 7104:7120]
        SUDP = Gf[:, 7168:7232]
        SUC = Gf[:, 7232:7248]
        ps_free = [None] * 8
        ps_ready = [None] * 8

        p.st("gpsimd", lambda e: e.memset(G[:, :], 0.0), inc=(eG, 1))
        tz_g = p.tok(eG)
        p.st("gpsimd", lambda e: e.memset(ZT[:, :], 0.0), inc=(eG, 1))
        tz_zt = p.tok(eG)
        p.st("gpsimd", lambda e: e.memset(SPAD[:, :], 0.0), inc=(eG, 1))
        tz_sp = p.tok(eG)
        p.st("gpsimd", lambda e: e.memset(GP[:, :], 0.0), inc=(eG, 1))
        tz_gp = p.tok(eG)
        p.st("gpsimd", lambda e: e.memset(NONESB[:, :], -1.0 / 2048.0), inc=(eG, 1))
        p.st("gpsimd", lambda e: e.memset(PONES[:, :], 1.0 / 2048.0), inc=(eG, 1))
        p.st("gpsimd", lambda e: e.memset(ZT[0:1, 1024:1152], 1.0),
             waits=[tz_zt], inc=(eG, 1))
        t_memset = p.tok(eG)

        def dS(out_ap, in_ap, waits=()):
            return dma("sync", QS,
                       lambda e, o=out_ap, i=in_ap: e.dma_start(out=o, in_=i),
                       waits)

        t_su = None
        t_pt = dS(PT[0:2, :], pmat, [tz_g])
        t_at = dS(ATC[0:2, :], a2, [tz_g])
        t_wbs = dS(WBS[:, :].rearrange("p (r c) -> p r c", r=2),
                   wbsT.rearrange("(r p) c -> p r c", p=128), [tz_g])
        t_wcr = dS(WCR[:, :].rearrange("p (r c) -> p r c", r=2),
                   wcr.rearrange("(r p) c -> p r c", p=128), [tz_g])
        t_wcm = dS(WCM[0:64, :], wcm, [tz_g])
        t_wscu = dS(WSCU[:, :], wscu_d)
        t_wscg = dS(WSCG[:, :], wscg_d)
        t_sudp = dS(SUDP, sudp_d, [tz_g])
        t_wsdp = dS(WSC05[:, :], wscdp_d)
        t_dcol = dS(DCOL[:, :], dcol_d)
        t_suc = dS(SUC, sucols, [tz_g])

        t_h128 = dma("gpsimd", QG,
                     lambda e: e.dma_start(out=H128B[:, :], in_=h128), [t_memset])
        t_wbrb = dma("gpsimd", QG,
                     lambda e: e.dma_start(
                         out=WBRB[:, :].rearrange("p (c r) -> p c r", c=NDT),
                         in_=wbr.rearrange("(c p) r -> p c r", p=128)),
                     [t_memset])

        p.st("vector", lambda e: e.tensor_tensor(WSCU[:, :], WSCU[:, :], SUDP, OP.mult),
             waits=[t_wscu, t_sudp], inc=(eV, 1))
        p.st("vector", lambda e: e.tensor_single_scalar(WSCU[:, :], WSCU[:, :], C8192, OP.mult),
             waits=[p.tok(eV)], inc=(eV, 1))
        p.st("vector", lambda e: e.tensor_single_scalar(WSC05[:, :], WSC05[:, :], 0.5, OP.mult),
             waits=[t_wsdp], inc=(eV, 1))
        t_vecprep = p.tok(eV)

        for l in range(8):
            lo = l * 256
            p.st("tensor",
                 lambda e, l=l, lo=lo: e.matmul(
                     P4[0:2, 0:256], ATC[:, 2 * l:2 * l + 2], PT[:, lo:lo + 256],
                     start=True, stop=True),
                 waits=[t_pt, t_at, ps_free[4]], inc=(ePE, 1))
            ps_ready[4] = p.tok(ePE)
            p.st("vector",
                 lambda e, lo=lo: e.tensor_copy(C1[0:2, lo:lo + 256], P4[0:2, 0:256]),
                 waits=[ps_ready[4]], inc=(eV, 1))
            ps_free[4] = p.tok(eV)
            t_c1 = p.tok(eV)
            for h in range(2):
                p.st("tensor",
                     lambda e, lo=lo, h=h: e.matmul(
                         P5[:, h * 256:(h + 1) * 256],
                         C1[:, lo + h * 128:lo + (h + 1) * 128],
                         PT[:, lo:lo + 256], start=True, stop=True),
                     waits=[t_c1, ps_free[5]], inc=(ePE, 1))
            ps_ready[5] = p.tok(ePE)
            for h in range(2):
                if l == 0:
                    p.st("vector",
                         lambda e, h=h: e.tensor_copy(
                             MC[:, h * 256:(h + 1) * 256], P5[:, h * 256:(h + 1) * 256]),
                         waits=[ps_ready[5], tz_g], inc=(eV, 1))
                else:
                    p.st("vector",
                         lambda e, h=h: e.tensor_tensor(
                             MC[:, h * 256:(h + 1) * 256], MC[:, h * 256:(h + 1) * 256],
                             P5[:, h * 256:(h + 1) * 256], OP.mult),
                         waits=[ps_ready[5], p.tok(eV)], inc=(eV, 1))
            ps_free[5] = p.tok(eV)
        t_m = p.tok(eV)

        for h in range(2):
            for r in range(2):
                p.st("tensor",
                     lambda e, h=h, r=r: e.matmul(
                         P4[:, h * 64:(h + 1) * 64],
                         MC[:, r * 256 + h * 128:r * 256 + (h + 1) * 128],
                         WBS[:, r * 64:(r + 1) * 64],
                         start=(r == 0), stop=(r == 1)),
                     waits=[t_m, t_wbs, ps_free[4]], inc=(ePE, 1))
        ps_ready[4] = p.tok(ePE)
        p.st("vector", lambda e: e.tensor_copy(T1C[:, :], P4[:, 0:128]),
             waits=[ps_ready[4]], inc=(eV, 1))
        ps_free[4] = p.tok(eV)
        t_t1 = p.tok(eV)

        for r in range(2):
            p.st("tensor",
                 lambda e, r=r: e.matmul(
                     P5[0:64, 0:64], WCR[:, r * 64:(r + 1) * 64],
                     T1C[:, r * 64:(r + 1) * 64], start=(r == 0), stop=(r == 1)),
                 waits=[t_t1, t_wcr, ps_free[5]], inc=(ePE, 1))
        ps_ready[5] = p.tok(ePE)
        p.st("vector", lambda e: e.tensor_copy(KTC[0:64, :], P5[0:64, 0:64]),
             waits=[ps_ready[5], tz_g], inc=(eV, 1))
        ps_free[5] = p.tok(eV)
        t_kt = p.tok(eV)

        for q in range(4):
            b = 6 + (q % 2)
            p.st("tensor",
                 lambda e, q=q, b=b: e.matmul(
                     PS[b][0:64, :], KTC[:, :], WCM[:, q * 512:(q + 1) * 512],
                     start=True, stop=True),
                 waits=[t_kt, t_wcm, ps_free[b]], inc=(ePE, 1))
            ps_ready[b] = p.tok(ePE)
            p.st("vector",
                 lambda e, q=q, b=b: e.tensor_copy(
                     GP[0:64, q * 512:(q + 1) * 512], PS[b][0:64, :]),
                 waits=[ps_ready[b], tz_gp], inc=(eV, 1))
            ps_free[b] = p.tok(eV)
        t_gp = p.tok(eV)

        p.st("gpsimd", lambda e: e.memset(SROWS[:, :], 0.0), inc=(eG, 1))
        tz_srows = p.tok(eG)
        p.st("gpsimd", lambda e: e.memset(NMRB[:, :], 0.0), inc=(eG, 1))
        tz_nmrb = p.tok(eG)
        p.st("vector", lambda e: e.tensor_copy(SUCB[:, :], SUC),
             waits=[t_suc], inc=(eV, 1))
        t_sucb = p.tok(eV)
        for dt in range(NDT):
            p.st("vector",
                 lambda e, dt=dt: e.tensor_scalar(
                     HP[:, dt * 128:(dt + 1) * 128], H128B[:, :],
                     SUC[:, dt:dt + 1], None, OP.mult),
                 waits=[t_suc, t_h128] if dt == 0 else [], inc=(eV, 1))
        t_hp = p.tok(eV)
        for g in range(4):
            for k in range(4):
                dt = g * 4 + k
                p.st("tensor",
                     lambda e, dt=dt, k=k: e.matmul(
                         P6[0:1, k * 128:(k + 1) * 128], SUCB[:, dt:dt + 1],
                         H128B[:, :], start=True, stop=True),
                     waits=([t_sucb, t_h128] if dt == 0 else [])
                     + ([ps_free[6]] if k == 0 else []),
                     inc=(ePE, 1) if k == 3 else None)
            ps_ready[6] = p.tok(ePE)
            p.st("vector",
                 lambda e, g=g: e.tensor_copy(
                     SROWS[0:1, g * 512:(g + 1) * 512], P6[0:1, 0:512]),
                 waits=[ps_ready[6], tz_srows], inc=(eV, 1))
            ps_free[6] = p.tok(eV)
        t_srows = p.tok(eV)

        wupTr = wupT.rearrange("(c p) i -> p c i", p=128)
        wgaTr = wgaT.rearrange("(c p) i -> p c i", p=128)
        wdnTr = wdnT.rearrange("(c p) d -> p c d", p=128)
        WD0 = WD[:, 0:NIT * 128]
        WD1 = WD[:, NIT * 128:2 * NIT * 128]
        wb_up_tok = [None] * NIT
        wb_ga_tok = [None] * NIT
        wb_dn_tok = [None] * NDT

        uTr = uT.rearrange("(c p) t -> p c t", p=128)
        outTr = outT.rearrange("(c p) t -> p c t", p=128)
        Xr = X[:, :].rearrange("p (c t) -> p c t", c=NDT)

        t_store_prev = None
        t_x_ready = None
        t_FW_free = (p.tok(ePE), p.tok(eV))
        t_LN_free = (p.tok(eV), p.tok(eV))
        t_G_free = p.tok(ePE)
        t_rep_prev = p.tok(ePE)
        t_stat_evac_prev = p.tok(eS)
        t_norm_prev = p.tok(eV)
        t_p23_free = p.tok(eV)
        t_nmr_prev = p.tok(eV)
        t_spad_free = p.tok(ePE)
        t_srep_free = p.tok(eV)
        par_up_free = [p.tok(ePE), p.tok(ePE)]
        sil_free = [p.tok(eV), p.tok(eV)]

        fold_state = {}

        def emit_fold_cast(dt):
            fold_state["castd"] = dma(
                "gpsimd", QG,
                lambda e, dt=dt: e.dma_start(
                    out=WD0.rearrange("p (c d) -> p c d", c=NIT),
                    in_=wdnTr[:, :, dt * 128:(dt + 1) * 128]),
                [fold_state.get("p6"), t_h128 if dt == 0 else None])

        def emit_fold_compute(dt):
            nonlocal t_p23_free
            t_castd = fold_state["castd"]
            t_wb_prev = wb_dn_tok[dt - 1] if dt > 0 else None
            for grp in range(16):
                b = 2 + grp % 2
                p.st("tensor",
                     lambda e, b=b, grp=grp: e.matmul(
                         PS[b][:, :], H128B[:, :],
                         WD0[:, grp * 512:(grp + 1) * 512], start=True, stop=True),
                     waits=([t_castd, t_p23_free, t_rep_prev] if grp == 0 else [])
                     + ([ps_free[b]] if grp >= 2 else []),
                     inc=(ePE, 1))
                ps_ready[b] = p.tok(ePE)
                p.st("scalar",
                     lambda e, b=b, grp=grp: e.copy(
                         WD1[:, grp * 512:(grp + 1) * 512], PS[b][:, :]),
                     waits=[ps_ready[b], t_wb_prev if grp == 0 else None],
                     inc=(eS, 1))
                ps_free[b] = p.tok(eS)
            t_stage1_pe = p.tok(ePE)
            t_p23_free = p.tok(eV)
            srcdst = [(WD1, WD0), (WD0, WD1)]
            t_ev_fold = p.tok(eS)
            tV = None
            for pi, dc in enumerate([32, 16, 8, 4, 2, 1]):
                src, dst = srcdst[pi % 2]
                nblk = 64 // (2 * dc)
                first = True
                for bk in range(nblk):
                    a0 = (2 * bk) * dc * 128
                    b0 = (2 * bk + 1) * dc * 128
                    w = dc * 128
                    for op in (OP.add, OP.subtract):
                        dsto = a0 if op == OP.add else b0
                        p.st("vector",
                             lambda e, src=src, dst=dst, a0=a0, b0=b0, w=w, op=op,
                             dsto=dsto: e.tensor_tensor(
                                 dst[:, dsto:dsto + w], src[:, a0:a0 + w],
                                 src[:, b0:b0 + w], op),
                             waits=([tV, t_ev_fold] + ([t_stage1_pe] if pi == 0 else []))
                             if first else [],
                             inc=(eV, 1))
                        first = False
                tV = p.tok(eV)
            fold_state["p6"] = tV
            wb_dn_tok[dt] = dma(
                "scalar", QA,
                lambda e, dt=dt: e.dma_start(out=scr_dn[dt], in_=WD1), [tV])

        pre_ln = {}

        def emit_sq(dt, waits, ln_waits=()):
            p.st("scalar",
                 lambda e, dt=dt: e.activation(
                     FW[:, dt * TC:(dt + 1) * TC], Xr[:, dt, :], AF.Square),
                 waits=waits, inc=(eS, 1))
            tsq = p.tok(eS)
            p.st("vector",
                 lambda e, dt=dt: e.tensor_copy(
                     LN[:, dt * TC:(dt + 1) * TC], Xr[:, dt, :]),
                 waits=list(waits) + list(ln_waits), inc=(eV, 1))
            return (tsq, p.tok(eV))

        def emit_stat_pair(dt, sq_tok):
            nonlocal t_stat_evac_prev
            p.st("tensor",
                 lambda e, dt=dt: e.matmul(
                     P0[0:1, :], NONESB[:, :], LN[:, dt * TC:(dt + 1) * TC],
                     start=(dt == 0), stop=(dt == NDT - 1)),
                 waits=[sq_tok[dt][1]] + ([t_stat_evac_prev, t_memset] if dt == 0 else []))
            p.st("tensor",
                 lambda e, dt=dt: e.matmul(
                     P1[0:1, :], PONES[:, :], FW[:, dt * TC:(dt + 1) * TC],
                     start=(dt == 0), stop=(dt == NDT - 1)),
                 waits=[sq_tok[dt][0]],
                 inc=(ePE, 1) if dt == NDT - 1 else None)
            if dt == NDT - 1:
                pre_ln["t_stats"] = p.tok(ePE)

        def emit_ln_common(sq_tok, cmul):
            nonlocal t_stat_evac_prev, t_nmr_prev
            for c2 in range(NDT):
                emit_stat_pair(c2, sq_tok)
            t_stats = pre_ln["t_stats"]
            p.st("scalar", lambda e: e.copy(STATA[0:1, 0:TC], P0[0:1, :]),
                 waits=[t_stats, t_nmr_prev], inc=(eS, 1))
            p.st("scalar", lambda e: e.copy(STATA[0:1, TC:2 * TC], P1[0:1, :]),
                 waits=[t_stats], inc=(eS, 1))
            t_stat_evac_prev = p.tok(eS)
            p.st("vector", lambda e: e.tensor_tensor(
                ZT[0:1, 0:512], STATA[0:1, 0:TC], STATA[0:1, 0:TC], OP.mult),
                 waits=[t_stat_evac_prev, t_rep_prev], inc=(eV, 1))
            p.st("vector", lambda e: e.tensor_tensor(
                STATA[0:1, TC:2 * TC], STATA[0:1, TC:2 * TC],
                ZT[0:1, 0:512], OP.subtract),
                 waits=[p.tok(eV)], inc=(eV, 1))
            p.st("vector", lambda e: e.tensor_single_scalar(
                STATA[0:1, TC:2 * TC], STATA[0:1, TC:2 * TC], EPS, OP.add),
                 waits=[p.tok(eV)], inc=(eV, 1))
            t_var = p.tok(eV)
            p.st("scalar", lambda e: e.activation(
                ZT[0:1, 0:512], STATA[0:1, TC:2 * TC], AF.Ln),
                 waits=[t_var], inc=(eS, 1))
            t_ln = p.tok(eS)
            p.st("vector", lambda e: e.tensor_single_scalar(
                ZT[0:1, 0:512], ZT[0:1, 0:512], -0.5, OP.mult),
                 waits=[t_ln], inc=(eV, 1))
            t_half = p.tok(eV)
            p.st("scalar", lambda e: e.activation(
                STATA[0:1, TC:2 * TC], ZT[0:1, 0:512], AF.Exp),
                 waits=[t_half], inc=(eS, 1))
            t_exp = p.tok(eS)
            p.st("vector", lambda e: e.tensor_single_scalar(
                ZT[0:1, 0:512], STATA[0:1, TC:2 * TC],
                C2048 if cmul else 1.0, OP.mult),
                 waits=[t_exp], inc=(eV, 1))

        def emit_ln_ffn(pre):
            nonlocal t_rep_prev, t_nmr_prev, t_p23_free, t_norm_prev, t_srep_free
            if not pre:
                sq_tok = [None] * NDT
                for c2 in range(NDT):
                    sq_tok[c2] = emit_sq(
                        c2, [t_x_ready, t_FW_free[0], t_FW_free[1]] if c2 == 0 else [],
                        [t_LN_free[0], t_LN_free[1]] if c2 == 0 else ())
            else:
                sq_tok = pre_ln["sq_tok"]
            emit_ln_common(sq_tok, cmul=True)
            t_rstd = p.tok(eV)
            p.st("scalar", lambda e: e.copy(NMRB[0:1, :], P0[0:1, :]),
                 waits=[pre_ln["t_stats"], t_rep_prev, tz_nmrb], inc=(eS, 1))
            t_nmrb = p.tok(eS)
            t_nmr_prev = t_rstd
            p.st("tensor", lambda e: e.matmul(
                P2[:, :], ONE1, RST, start=True, stop=True),
                 waits=[t_rstd, t_p23_free, t_norm_prev, t_su, t_memset],
                 inc=(ePE, 1))
            t_rep_prev = p.tok(ePE)
            p.st("vector", lambda e: e.tensor_copy(SREP[:, :], P2[:, :]),
                 waits=[t_rep_prev, t_srep_free], inc=(eV, 1))
            t_srep = p.tok(eV)
            t_p23_free = t_srep
            t_norm_prev = t_srep
            return sq_tok, t_nmrb, t_srep

        def emit_ln_mix(pre):
            nonlocal t_rep_prev, t_nmr_prev, t_p23_free, t_norm_prev
            if not pre:
                sq_tok = [None] * NDT
                for c2 in range(NDT):
                    sq_tok[c2] = emit_sq(
                        c2, [t_x_ready, t_FW_free[0], t_FW_free[1]] if c2 == 0 else [],
                        [t_LN_free[0], t_LN_free[1]] if c2 == 0 else ())
            else:
                sq_tok = pre_ln["sq_tok"]
            emit_ln_common(sq_tok, cmul=False)
            t_rstd = p.tok(eV)
            p.st("vector", lambda e: e.tensor_tensor(
                ZT[0:1, 512:1024], STATA[0:1, 0:TC], ZT[0:1, 0:512], OP.mult),
                 waits=[t_rstd, t_rep_prev], inc=(eV, 1))
            t_nmr_prev = p.tok(eV)
            p.st("tensor", lambda e: e.matmul(
                P2[:, :], ONE1, RST, start=True, stop=True),
                 waits=[t_nmr_prev, t_p23_free, t_norm_prev, t_su, t_memset])
            p.st("tensor", lambda e: e.matmul(
                P3[:, :], ONE1, NMR, start=True, stop=True), inc=(ePE, 1))
            t_rep_prev = p.tok(ePE)
            tile_tok = [None] * NDT
            for dtile in range(NDT):
                p.st("vector",
                     lambda e, dtile=dtile: e.scalar_tensor_tensor(
                         LN[:, dtile * TC:(dtile + 1) * TC], Xr[:, dtile, :],
                         1.0, P2[:, :], OP.mult, OP.mult),
                     waits=[t_rep_prev], inc=(eV, 1))
                p.st("vector",
                     lambda e, dtile=dtile: e.scalar_tensor_tensor(
                         LN[:, dtile * TC:(dtile + 1) * TC], P3[:, :],
                         1.0, LN[:, dtile * TC:(dtile + 1) * TC], OP.mult, OP.add),
                     waits=[p.tok(eV)], inc=(eV, 1))
                tile_tok[dtile] = p.tok(eV)
            t_norm_prev = p.tok(eV)
            t_p23_free = t_norm_prev
            return tile_tok

        def emit_ffn(do_prep, pre=False, fuse_next=False):
            nonlocal t_FW_free, t_LN_free, t_G_free, t_x_ready
            nonlocal par_up_free, sil_free, t_srep_free
            sq_tok, t_nmrb, t_srep = emit_ln_ffn(pre)
            st1_evac = [None] * 4
            for c2 in range(NDT):
                b = 4 + c2 % 4
                p.st("tensor",
                     lambda e, b=b, c2=c2: e.matmul(
                         PS[b][:, :], HP[:, c2 * 128:(c2 + 1) * 128],
                         LN[:, c2 * TC:(c2 + 1) * TC], start=True, stop=False),
                     waits=[sq_tok[c2][1], t_hp if c2 == 0 else None,
                            ps_free[b] if c2 < 4 else st1_evac[c2 % 4]])
                p.st("tensor",
                     lambda e, b=b, c2=c2: e.matmul(
                         PS[b][:, :], SROWS[:, c2 * 128:(c2 + 1) * 128],
                         NMRB[:, :], start=False, stop=True),
                     waits=[t_nmrb, t_srows] if c2 == 0 else [],
                     inc=(ePE, 1))
                t_mm = p.tok(ePE)
                p.st("vector",
                     lambda e, b=b, c2=c2: e.tensor_tensor(
                         FW[:, c2 * TC:(c2 + 1) * TC], PS[b][:, :], SREP[:, :],
                         OP.mult),
                     waits=[t_mm, t_srep if c2 == 0 else None], inc=(eV, 1))
                st1_evac[c2 % 4] = p.tok(eV)
                ps_free[b] = p.tok(eV)
            t_st1_pe = p.tok(ePE)
            t_st1_v = p.tok(eV)
            t_srep_free = t_st1_v
            tV = t_st1_v
            srcdst = [(FW, LN), (LN, FW)]
            for pi, dc in enumerate([8, 4, 2, 1]):
                src, dst = srcdst[pi % 2]
                nblk = NDT // (2 * dc)
                first = True
                for bk in range(nblk):
                    a0 = (2 * bk) * dc * TC
                    b0 = (2 * bk + 1) * dc * TC
                    w = dc * TC
                    for op in (OP.add, OP.subtract):
                        dsto = a0 if op == OP.add else b0
                        p.st("vector",
                             lambda e, src=src, dst=dst, a0=a0, b0=b0, w=w, op=op,
                             dsto=dsto: e.tensor_tensor(
                                 dst[:, dsto:dsto + w], src[:, a0:a0 + w],
                                 src[:, b0:b0 + w], op),
                             waits=([tV] + ([t_st1_pe] if pi == 0 else []))
                             if first else [],
                             inc=(eV, 1))
                        first = False
                tV = p.tok(eV)
            t_fw_ready = tV
            up_loaded = [None] * 2
            ga_loaded = [None] * 2

            def emit_cast_pair(ic):
                wb_up_tok[ic] = dma(
                    "gpsimd", QG,
                    lambda e, ic=ic: e.dma_start(
                        out=scr_ug[ic][:, 0:NDT * 128],
                        in_=wupTr[:, :, ic * 128:(ic + 1) * 128]), [])
                wb_ga_tok[ic] = dma(
                    "gpsimd", QG,
                    lambda e, ic=ic: e.dma_start(
                        out=scr_ug[ic][:, NDT * 128:2 * NDT * 128],
                        in_=wgaTr[:, :, ic * 128:(ic + 1) * 128]), [])

            def upgate_loads(ic):
                pp = ic % 2
                slug = WUG[:, pp * 4096:(pp + 1) * 4096]
                up_loaded[pp] = dma(
                    "sync", QS,
                    lambda e, slug=slug, ic=ic: e.dma_start(
                        out=slug, in_=scr_ug[ic]),
                    [par_up_free[pp], wb_up_tok[ic], wb_ga_tok[ic]])
                ga_loaded[pp] = up_loaded[pp]

            t_upga = par_up_free[0]
            for it in range(NIT + 2):
                if it < 2:
                    if do_prep and it == 0:
                        for j in range(2):
                            emit_cast_pair(j)
                        emit_fold_cast(0)
                        for j in range(2, 6):
                            emit_cast_pair(j)
                    upgate_loads(it)
                    continue
                ic = it - 2
                pp = ic % 2
                slu = WUG[:, pp * 4096:pp * 4096 + 2048]
                slg = WUG[:, pp * 4096 + 2048:(pp + 1) * 4096]
                bU = 4 if pp == 0 else 6
                bG = 5 if pp == 0 else 7
                for c2 in range(NDT):
                    p.st("tensor",
                         lambda e, bU=bU, slu=slu, c2=c2: e.matmul(
                             PS[bU][:, :], slu[:, c2 * 128:(c2 + 1) * 128],
                             FW[:, c2 * TC:(c2 + 1) * TC],
                             start=(c2 == 0), stop=(c2 == NDT - 1)),
                         waits=[up_loaded[pp], t_fw_ready,
                                ps_free[bU]] if c2 == 0 else [])
                for c2 in range(NDT):
                    p.st("tensor",
                         lambda e, bG=bG, slg=slg, c2=c2: e.matmul(
                             PS[bG][:, :], slg[:, c2 * 128:(c2 + 1) * 128],
                             FW[:, c2 * TC:(c2 + 1) * TC],
                             start=(c2 == 0), stop=(c2 == NDT - 1)),
                         waits=[ga_loaded[pp], ps_free[bG]] if c2 == 0 else [],
                         inc=(ePE, 1) if c2 == NDT - 1 else None)
                t_upga = p.tok(ePE)
                par_up_free[pp] = t_upga
                ssl = SIL[:, 0:TC]
                p.st("scalar",
                     lambda e, ssl=ssl, bG=bG, ic=ic: e.activation(
                         ssl, PS[bG][:, :], AF.Silu, scale=WSCG[:, ic:ic + 1]),
                     waits=[t_upga, sil_free[0], t_wscg if ic == 0 else None],
                     inc=(eS, 1))
                t_sil = p.tok(eS)
                ps_free[bG] = t_sil
                p.st("vector",
                     lambda e, bU=bU, ssl=ssl, ic=ic: e.scalar_tensor_tensor(
                         G[:, ic * TC:(ic + 1) * TC], PS[bU][:, :],
                         WSCU[:, ic:ic + 1], ssl, OP.mult, OP.mult),
                     waits=[t_sil, t_upga, t_vecprep if ic == 0 else None,
                            t_G_free if ic == 0 else None],
                     inc=(eV, 1))
                sil_free[0] = p.tok(eV)
                ps_free[bU] = p.tok(eV)
                if do_prep and ic % 4 == 3:
                    emit_fold_compute(ic // 4)
                    if ic // 4 + 1 < NDT:
                        emit_fold_cast(ic // 4 + 1)
                if do_prep and ic + 6 < NIT:
                    emit_cast_pair(ic + 6)
                if it < NIT:
                    upgate_loads(it)
            t_g_ready = p.tok(eV)
            t_FW_free = (t_upga, p.tok(eV))
            t_LN_free = (t_fw_ready, t_fw_ready)
            if do_prep:
                emit_ffn.par_dn = [fold_state["p6"], wb_dn_tok[NDT - 1]]
            dn_loaded = [None] * 2

            def dn_load(dtile):
                pp = dtile % 2
                sld = WD[:, pp * 8192:(pp + 1) * 8192]
                dn_loaded[pp] = dma(
                    "sync", QS,
                    lambda e, sld=sld, dtile=dtile: e.dma_start(
                        out=sld, in_=scr_dn[dtile]),
                    [emit_ffn.par_dn[pp], wb_dn_tok[dtile]])

            t_dn = emit_ffn.par_dn[0]
            sq_tok = [None] * NDT
            for it in range(NDT + 2):
                if it < 2:
                    dn_load(it)
                    continue
                dtile = it - 2
                pp = dtile % 2
                sld = WD[:, pp * 8192:(pp + 1) * 8192]
                b = 4 + pp
                for c in range(NIT):
                    p.st("tensor",
                         lambda e, b=b, sld=sld, c=c: e.matmul(
                             PS[b][:, :], sld[:, c * 128:(c + 1) * 128],
                             G[:, c * TC:(c + 1) * TC],
                             start=(c == 0), stop=(c == NIT - 1)),
                         waits=[dn_loaded[pp], t_g_ready, ps_free[b]] if c == 0 else [],
                         inc=(ePE, 1) if c == NIT - 1 else None)
                t_dn = p.tok(ePE)
                emit_ffn.par_dn[pp] = t_dn
                p.st("vector",
                     lambda e, b=b, dtile=dtile: e.scalar_tensor_tensor(
                         Xr[:, dtile, :], PS[b][:, :], WSC05[:, dtile:dtile + 1],
                         Xr[:, dtile, :], OP.mult, OP.add),
                     waits=[t_dn, t_vecprep if dtile == 0 else None], inc=(eV, 1))
                ps_free[b] = p.tok(eV)
                if fuse_next:
                    sq_tok[dtile] = emit_sq(
                        dtile, [p.tok(eV)] + ([t_FW_free[0], t_FW_free[1]]
                                             if dtile == 0 else []),
                        [t_LN_free[0], t_LN_free[1]] if dtile == 0 else ())
                if it < NDT:
                    dn_load(it)
            if fuse_next:
                pre_ln["sq_tok"] = sq_tok
            t_x_ready = p.tok(eV)
            t_G_free = t_dn

        emit_ffn.par_dn = [p.tok(ePE), p.tok(ePE)]

        def emit_mixer(pre=False, fuse_next=False):
            nonlocal t_FW_free, t_LN_free, t_x_ready, t_spad_free
            tile_tok = emit_ln_mix(pre)
            for c2 in range(NDT):
                p.st("scalar",
                     lambda e, c2=c2: e.activation(
                         FW[:, c2 * TC:(c2 + 1) * TC], LN[:, c2 * TC:(c2 + 1) * TC],
                         AF.Copy, scale=DCOL[:, c2:c2 + 1]),
                     waits=[tile_tok[c2]] + ([t_dcol, t_FW_free[0], t_FW_free[1]]
                                             if c2 == 0 else []),
                     inc=(eS, 1) if c2 == NDT - 1 else None)
            t_dln = p.tok(eS)
            for c2 in range(NDT):
                p.st("tensor",
                     lambda e, c2=c2: e.matmul(
                         P6[0:64, :], WBRB[:, c2 * 64:(c2 + 1) * 64],
                         LN[:, c2 * TC:(c2 + 1) * TC],
                         start=(c2 == 0), stop=(c2 == NDT - 1)),
                     waits=[tile_tok[c2]] + ([t_wbrb, ps_free[6]] if c2 == 0 else []),
                     inc=(ePE, 1) if c2 == NDT - 1 else None)
            t_s0 = p.tok(ePE)
            p.st("scalar", lambda e: e.copy(SPAD[0:64, :], P6[0:64, :]),
                 waits=[t_s0, t_spad_free, tz_sp], inc=(eS, 1))
            t_spad = p.tok(eS)
            ps_free[6] = t_spad
            t_y = t_s0
            sq_tok = [None] * NDT
            for dtile in range(NDT):
                b = 4 + dtile % 2
                p.st("tensor",
                     lambda e, b=b, dtile=dtile: e.matmul(
                         PS[b][:, :], GP[:, dtile * 128:(dtile + 1) * 128],
                         SPAD[:, :], start=True, stop=True),
                     waits=[t_spad, t_gp if dtile == 0 else None, ps_free[b]],
                     inc=(ePE, 1))
                t_y = p.tok(ePE)
                emit_ffn.par_dn[dtile % 2] = t_y
                p.st("vector",
                     lambda e, b=b, dtile=dtile: e.scalar_tensor_tensor(
                         Xr[:, dtile, :], PS[b][:, :], 1.0, Xr[:, dtile, :],
                         OP.mult, OP.add),
                     waits=[t_y], inc=(eV, 1))
                p.st("vector",
                     lambda e, dtile=dtile: e.tensor_tensor(
                         Xr[:, dtile, :], Xr[:, dtile, :],
                         FW[:, dtile * TC:(dtile + 1) * TC], OP.add),
                     waits=[t_dln, p.tok(eV)], inc=(eV, 1))
                ps_free[b] = p.tok(eV)
                if fuse_next:
                    sq_tok[dtile] = emit_sq(dtile, [p.tok(eV)],
                                            [t_dln] if dtile == 0 else ())
            if fuse_next:
                pre_ln["sq_tok"] = sq_tok
            t_x_ready = p.tok(eV)
            t_spad_free = t_y
            t_FW_free = (t_s0, t_x_ready)
            t_LN_free = (t_x_ready, t_dln)

        for ch in range(NCH):
            t_x_ready = dma(
                "sync", QS,
                lambda e, ch=ch: e.dma_start(
                    out=Xr, in_=uTr[:, :, ch * TC:(ch + 1) * TC]),
                [t_store_prev, t_x_ready])
            emit_ffn(do_prep=(ch == 0), fuse_next=True)
            emit_mixer(pre=True, fuse_next=True)
            emit_ffn(do_prep=False, pre=True)
            t_store_prev = dma(
                "sync", QS,
                lambda e, ch=ch: e.dma_start(
                    out=outTr[:, :, ch * TC:(ch + 1) * TC], in_=Xr),
                [t_x_ready])

        p.st("sync", None, waits=[p.tok(qS), p.tok(qS2)])
        p.st("scalar", None, waits=[p.tok(qA), p.tok(qA2)])
        p.st("gpsimd", None, waits=[p.tok(qG), p.tok(qG2)])

        prog = p.steps

        for eng_name in ["sync", "scalar", "vector", "gpsimd", "tensor"]:
            def mk(nm):
                def body(eng):
                    for (e, fn, ws, inc) in prog:
                        if e != nm:
                            continue
                        for (sem, val) in ws:
                            eng.wait_ge(sem, val)
                        if fn is not None:
                            ins = fn(eng)
                            if inc is not None:
                                ins.then_inc(inc[0], inc[1])
                return body
            getattr(block, eng_name)(mk(eng_name))

    return nc


def _prep_host(inputs):
    f32 = np.float32
    u = np.asarray(inputs["L0_attn_u"], dtype=f32)
    u2 = np.ascontiguousarray(u.reshape(16384, 2048))

    WupT = np.ascontiguousarray(np.asarray(inputs["L0_ffn_W_up"], f32).T)
    WgaT = np.ascontiguousarray(np.asarray(inputs["L0_ffn_W_gate"], f32).T)
    WdnT = np.ascontiguousarray(np.asarray(inputs["L0_ffn_W_down"], f32).T)
    wsc = np.asarray(inputs["L0_ffn_Wscale_ug"], f32)
    wscu = np.ascontiguousarray(wsc[:8192].reshape(64, 128).T)
    wscg = np.ascontiguousarray(wsc[8192:].reshape(64, 128).T)
    sudp = np.ascontiguousarray(np.asarray(inputs["L0_ffn_SU_dp"], f32).reshape(64, 128).T)
    wscdp = np.ascontiguousarray(np.asarray(inputs["L0_ffn_Wscale_dp"], f32).reshape(16, 128).T)
    su = np.asarray(inputs["L0_ffn_SU_ug"], f32)
    suflat = np.ascontiguousarray(su.reshape(1, 2048))
    sucols = np.ascontiguousarray(su.reshape(16, 128).T)
    dcol = np.ascontiguousarray(np.asarray(inputs["L0_attn_D_vec"], f32).reshape(16, 128).T)

    A = np.asarray(inputs["L0_attn_A_factors"], f32)
    a2 = np.ascontiguousarray(A.transpose(1, 0, 2).reshape(2, 16))
    n = np.arange(256)
    pm = np.zeros((2, 8, 256), f32)
    for l in range(8):
        bits = (n >> (7 - l)) & 1
        pm[0, l, :] = (bits == 0)
        pm[1, l, :] = (bits == 1)
    pmat = np.ascontiguousarray(pm.reshape(2, 2048))
    h128 = _had(128)

    wbr = np.ascontiguousarray(np.asarray(inputs["L0_attn_W_b_rank"], f32))
    wbsT = np.ascontiguousarray(np.asarray(inputs["L0_attn_W_b_state"], f32).T)
    wcr = np.ascontiguousarray(np.asarray(inputs["L0_attn_W_c_rank"], f32))
    wcm = np.ascontiguousarray(np.asarray(inputs["L0_attn_W_c_model"], f32))

    shared = dict(wupT=WupT, wgaT=WgaT, wdnT=WdnT, wbr=wbr, wbsT=wbsT, wcr=wcr,
                  wcm=wcm, a2=a2, pmat=pmat, h128=h128, suflat=suflat,
                  sucols=sucols, wscu=wscu, wscg=wscg, sudp=sudp, wscdp=wscdp,
                  dcol=dcol)
    in_maps = []
    for c in range(8):
        m = dict(shared)
        m["uT"] = np.ascontiguousarray(u2[c * 2048:(c + 1) * 2048].T)
        in_maps.append(m)
    return in_maps


def kernel(**inputs):
    if "nc" not in _NC_CACHE:
        _NC_CACHE["nc"] = build_nc()
    nc = _NC_CACHE["nc"]
    in_maps = _prep_host(inputs)
    res = run_bass_kernel_spmd(nc, in_maps, core_ids=list(range(8)))
    out2 = np.empty((16384, 2048), dtype=np.float32)
    for c in range(8):
        out2[c * 2048:(c + 1) * 2048] = res.results[c]["outT"].T
    return out2.reshape(4, 4096, 2048)
